# version 01 (final)
# baseline (speedup 1.0000x reference)
import math
import os
import numpy as np
import concourse.bass as bass
import concourse.mybir as mybir
from contextlib import ExitStack

F32 = mybir.dt.float32
BF16 = mybir.dt.bfloat16
Alu = mybir.AluOpType
Act = mybir.ActivationFunctionType
AX = mybir.AxisListType

B, S, D = 2, 8192, 2048
HN, HD = 16, 128
II = 8192
T = B * S
NCORE = 8
TC = T // NCORE
NT = TC // 128
GRP = 4
NHC = 4
NQT = S // 128
ATT_SCALE = 1.0 / math.sqrt(HD)
INV_LN2 = 1.0 / math.log(2.0)
LN_EPS = 1e-5

FLOOR_WJS = [128.0, 64.0, 32.0, 16.0, 8.0, 4.0, 2.0, 1.0]
EXP_BITS = [6, 5, 4, 3, 2, 1, 0]
ENGS = ("sy", "v", "s", "t", "g")


class Sem:
    def __init__(self, stack, nc, name):
        self.h = stack.enter_context(nc.semaphore(name))
        self.v = 0


DVE_DRAIN_METHS = frozenset((
    "tensor_scalar", "scalar_tensor_tensor", "tensor_tensor", "tensor_copy",
    "tensor_reduce", "reciprocal", "memset", "select", "copy_predicated"))


class R:

    def __init__(self):
        self.q = {e: [] for e in ENGS}

    def op(self, eng, meth, *incs, **kw):
        tgt = [(s.h, n) for (s, n) in incs]
        for (s, n) in incs:
            s.v += n
        if eng == "v" and meth in DVE_DRAIN_METHS:
            def run(e, meth=meth, kw=kw, tgt=tgt):
                getattr(e, meth)(**kw)
                d = e.drain()
                for h, n in tgt:
                    d.then_inc(h, n)
        else:
            def run(e, meth=meth, kw=kw, tgt=tgt):
                ins = getattr(e, meth)(**kw)
                for h, n in tgt:
                    ins.then_inc(h, n)
        self.q[eng].append(run)

    def wait(self, eng, sem, v=None):
        val = sem.v if v is None else v
        h = sem.h
        self.q[eng].append(lambda e, h=h, val=val: e.wait_ge(h, val))

    def dma(self, eng, sem, out, in_):
        self.op(eng, "dma_start", (sem, 16), out=out, in_=in_)

    def ts(self, out, in0, s1, s2=None, op0=Alu.mult, op1=None, accum_out=None, inc=None, eng="v"):
        kw = dict(out=out, in0=in0, scalar1=s1, scalar2=s2, op0=op0)
        if op1 is not None:
            kw["op1"] = op1
        if accum_out is not None:
            kw["accum_out"] = accum_out
        self.op(eng, "tensor_scalar", *(inc or []), **kw)

    def tt(self, out, in0, in1, op, inc=None, eng="v"):
        self.op(eng, "tensor_tensor", *(inc or []), out=out, in0=in0, in1=in1, op=op)

    def stt(self, out, in0, scalar, in1, op0, op1, inc=None, eng="v"):
        self.op(eng, "scalar_tensor_tensor", *(inc or []), out=out, in0=in0,
                scalar=scalar, in1=in1, op0=op0, op1=op1)

    def cp(self, out, in_, inc=None, eng="v"):
        self.op(eng, "tensor_copy", *(inc or []), out=out, in_=in_)

    def scp(self, out, in_, inc=None):
        self.op("s", "copy", *(inc or []), out=out, in_=in_)

    def act(self, out, in_, func, bias=None, scale=None, accum_out=None, inc=None):
        kw = dict(out=out, in_=in_, func=func)
        if bias is not None:
            kw["bias"] = bias
        if scale is not None:
            kw["scale"] = scale
        if accum_out is not None:
            kw["accum_out"] = accum_out
        self.op("s", "activation", *(inc or []), **kw)

    def mm(self, out, lhsT, rhs, start, stop, inc=None):
        self.op("t", "matmul", *(inc or []), out=out, lhsT=lhsT, rhs=rhs,
                start=start, stop=stop)

    def tr(self, out, in_, ident, inc=None):
        self.op("t", "transpose", *(inc or []), out=out, in_=in_, identity=ident)


class Builder:
    def __init__(self, stage=5):
        self.stage = stage
        self.nc = bass.Bass("TRN2", target_bir_lowering=False)
        self.stack = ExitStack()
        self.r = R()

    def sem(self, name):
        return Sem(self.stack, self.nc, name)

    def sb(self, stk, name, shape, dtype):
        return stk.enter_context(self.nc.sbuf_tensor(name, shape, dtype))

    def psum(self, stk, name, shape, dtype=F32):
        return stk.enter_context(self.nc.psum_tensor(name, shape, dtype))

    def make_quant_scratch(self, stk, Fmax, name="qs", weights=False):
        nbmax = Fmax // 32
        d = {}
        for n in ["am", "blk1", "blk2", "blk3", "blk4", "blk5"]:
            d[n] = self.sb(stk, f"{name}_{n}", [128, nbmax], F32)[:]
        for n in ["w", "z0", "z", "tmp"]:
            d[n] = self.sb(stk, f"{name}_{n}", [128, Fmax], F32)[:]
        for n in ["c4", "c5", "gh", "th", "m"]:
            d[n] = self.sb(stk, f"{name}_{n}", [128, Fmax], BF16)[:]
        if weights:
            for n in ["c4s", "c2s", "c1s"]:
                d[n] = self.sb(stk, f"{name}_{n}", [128, Fmax], BF16)[:]
        return d

    def emit_quant(self, qs, x_ap, out_ap, F, mfrac, max_norm, weights,
                   s_dve, s_act):
        r = self.r
        nb = F // 32
        M = 2.0 ** mfrac
        am, b1, b2, b3, b4, b5 = (qs[k] for k in ["am", "blk1", "blk2", "blk3", "blk4", "blk5"])
        w_, z0, z, tmp = (qs[k][:, 0:F] for k in ["w", "z0", "z", "tmp"])
        c4, c5, gh, th, mmb = (qs[k][:, 0:F] for k in ["c4", "c5", "gh", "th", "m"])
        am, b1, b2, b3, b4, b5 = (a[:, 0:nb] for a in (am, b1, b2, b3, b4, b5))
        x3 = x_ap.rearrange("p (b e) -> p b e", e=32)
        r.op("v", "tensor_reduce", out=am.unsqueeze(2), in_=x3, op=Alu.max,
             apply_absolute_value=True, axis=AX.X)
        r.ts(b1, am, 2.0 ** -60, op0=Alu.max, inc=[(s_dve, 1)])
        r.wait("s", s_dve)
        r.act(b2, b1, Act.Ln, inc=[(s_act, 1)])
        r.wait("v", s_act)
        r.ts(b1, b2, INV_LN2, -2.0, op0=Alu.mult, op1=Alu.add)
        r.ts(b2, b1, 130.0, op0=Alu.add)
        r.cp(b3, b2)
        for wj in FLOOR_WJS:
            r.ts(b4, b3, wj, -wj, op0=Alu.is_ge, op1=Alu.mult)
            r.tt(b3, b4, b3, Alu.add)
        r.tt(b4, b2, b3, Alu.subtract)
        r.ts(b1, b4, -130.0, op0=Alu.add)
        r.ts(b2, b1, -127.0, 127.0, op0=Alu.max, op1=Alu.min)
        r.ts(b3, b2, 0.0, op0=Alu.is_lt)
        r.ts(b4, b2, -1.0, op0=Alu.mult)
        r.tt(b4, b4, b2, Alu.max)
        r.op("v", "memset", ap=b1, constant=1.0)
        for i in EXP_BITS:
            wv = float(2 ** i)
            r.ts(b5, b4, wv, op0=Alu.is_ge)
            r.stt(b4, b5, -wv, b4, Alu.mult, Alu.add)
            r.ts(b5, b5, float(2.0 ** wv - 1.0), 1.0, op0=Alu.mult, op1=Alu.add)
            r.tt(b1, b1, b5, Alu.mult)
        r.op("v", "reciprocal", out=b2, in_=b1)
        r.tt(b4, b2, b1, Alu.subtract)
        r.tt(b5, b3, b4, Alu.mult)
        r.tt(b4, b1, b5, Alu.add)
        r.tt(b5, b1, b2, Alu.subtract)
        r.tt(b1, b3, b5, Alu.mult)
        r.tt(b2, b2, b1, Alu.add)
        r.ts(b2, b2, M, op0=Alu.mult)
        r.ts(b4, b4, 1.0 / M, op0=Alu.mult)
        rsm_b = b2.unsqueeze(2).broadcast_to((128, nb, 32))
        sm_b = b4.unsqueeze(2).broadcast_to((128, nb, 32))
        r3 = lambda a: a.rearrange("p (b e) -> p b e", e=32)
        r.ts(tmp, x_ap, -1.0, op0=Alu.mult)
        r.tt(tmp, tmp, x_ap, Alu.max)
        r.tt(r3(w_), r3(tmp), rsm_b, Alu.mult)
        r.ts(c4, w_, 2.0 * M, op0=Alu.is_ge)
        r.ts(c5, w_, 4.0 * M, op0=Alu.is_ge)
        r.ts(gh, c4, 0.5, 0.5, op0=Alu.mult, op1=Alu.add)
        r.stt(gh, c5, 1.0, gh, Alu.mult, Alu.add)
        r.tt(z0, w_, gh, Alu.add)
        r.cp(z, z0)
        wj = 4.0 * M
        while wj >= 4.0:
            r.ts(tmp, z, wj, op0=Alu.is_ge)
            if weights and wj == 4.0:
                r.cp(qs["c4s"][:, 0:F], tmp)
            r.stt(z, tmp, -wj, z, Alu.mult, Alu.add)
            wj /= 2.0
        r.ts(th, c5, 1.0e5, 2.0, op0=Alu.mult, op1=Alu.add)
        r.tt(tmp, z, th, Alu.is_ge)
        if weights:
            r.cp(qs["c2s"][:, 0:F], tmp)
        r.stt(z, tmp, -2.0, z, Alu.mult, Alu.add)
        r.ts(th, c4, 1.0e5, 1.0, op0=Alu.mult, op1=Alu.add)
        r.tt(tmp, z, th, Alu.is_ge)
        if weights:
            r.cp(qs["c1s"][:, 0:F], tmp)
        r.stt(z, tmp, -1.0, z, Alu.mult, Alu.add)
        r.tt(tmp, z0, z, Alu.subtract)
        if weights:
            c4s, c2s, c1s = (qs[k][:, 0:F] for k in ["c4s", "c2s", "c1s"])
            r.ts(th, z, 0.0, op0=Alu.is_le)
            r.tt(z, c2s, c1s, Alu.subtract)
            r.tt(z0, z, c4, Alu.mult)
            r.tt(z0, z0, c1s, Alu.add)
            r.tt(z, c4s, c2s, Alu.subtract)
            r.tt(z, z, c5, Alu.mult)
            r.tt(z0, z0, z, Alu.add)
            r.tt(z0, z0, th, Alu.mult)
            r.tt(z0, z0, gh, Alu.mult)
            r.stt(tmp, z0, -2.0, tmp, Alu.mult, Alu.add)
        r.ts(z, x_ap, 0.0, 2.0, op0=Alu.is_ge, op1=Alu.mult)
        r.stt(r3(mmb), r3(z), -1.0, sm_b, Alu.add, Alu.mult)
        r.ts(w_, tmp, max_norm * M, op0=Alu.min)
        r.tt(out_ap, w_, mmb, Alu.mult)

    def finish(self, dbg):
        r, nc = self.r, self.nc
        with nc.Block() as blk:
            for name, attr in (("sy", "sync"), ("v", "vector"), ("s", "scalar"),
                               ("t", "tensor"), ("g", "gpsimd")):
                q = r.q[name]
                if not q:
                    continue
                getattr(blk, attr)(lambda e, q=q: [f(e) for f in q])
        self.stack.close()
        return nc, dbg

    def emit_ln(self, ls, x_ap, out_ap, F, s_dve, s_act):
        r = self.r
        sm, ssq, var, r0, t1, scr = (ls[k] for k in ["sm", "ssq", "var", "r0", "t1", "scr"])
        r.op("v", "tensor_reduce", out=sm, in_=x_ap, op=Alu.add, axis=AX.X)
        r.ts(sm, sm, -1.0 / F, op0=Alu.mult, inc=[(s_dve, 1)])
        r.wait("s", s_dve)
        r.act(scr[:, 0:F], x_ap, Act.Square, accum_out=ssq, inc=[(s_act, 1)])
        r.wait("v", s_act)
        r.tt(var, sm, sm, Alu.mult)
        r.stt(var, ssq, 1.0 / F, var, Alu.mult, Alu.subtract)
        r.ts(var, var, LN_EPS, op0=Alu.add, inc=[(s_dve, 1)])
        r.wait("s", s_dve)
        r.act(r0, var, Act.Sqrt, inc=[(s_act, 1)])
        r.wait("v", s_act)
        r.op("v", "reciprocal", out=r0, in_=r0)
        r.tt(t1, r0, r0, Alu.mult)
        r.tt(t1, t1, var, Alu.mult)
        r.ts(t1, t1, -0.5, 1.5, op0=Alu.mult, op1=Alu.add)
        r.tt(r0, r0, t1, Alu.mult)
        r.op("v", "tensor_scalar", out=out_ap, in0=x_ap, scalar1=sm,
             scalar2=r0, op0=Alu.add, op1=Alu.mult)


def build(stage=5):
    bd = Builder(stage)
    nc, r, st = bd.nc, bd.r, bd.stack

    x_in = nc.dram_tensor("x_sl", [TC, D], F32, kind="ExternalInput").ap()
    wqkv_in = nc.dram_tensor("wqkv_sl", [D, 768], F32, kind="ExternalInput").ap()
    wo_in = nc.dram_tensor("wo_sl", [256, D], F32, kind="ExternalInput").ap()
    wg_in = nc.dram_tensor("wg_sl", [1024, D], F32, kind="ExternalInput").ap()
    wu_in = nc.dram_tensor("wu_sl", [1024, D], F32, kind="ExternalInput").ap()
    wd_in = nc.dram_tensor("wd_sl", [D, 1024], F32, kind="ExternalInput").ap()
    masks_in = nc.dram_tensor("masks_i", [128, 4, 512], F32, kind="ExternalInput").ap()
    identf_in = nc.dram_tensor("identf_i", [128, 128], F32, kind="ExternalInput").ap()
    identb_in = nc.dram_tensor("identb_i", [128, 128], F32, kind="ExternalInput").ap()
    y_out = nc.dram_tensor("y", [TC, D], F32, kind="ExternalOutput").ap()

    ag_g_in = nc.dram_tensor("ag_g_in", [16, 128, 1024], BF16)
    ag_u_in = nc.dram_tensor("ag_u_in", [16, 128, 1024], BF16)
    ag_d_in = nc.dram_tensor("ag_d_in", [8, 128, 2048], BF16)
    ag_g_out = nc.dram_tensor("ag_g_out", [128, 128, 1024], BF16)
    ag_u_out = nc.dram_tensor("ag_u_out", [128, 128, 1024], BF16)
    ag_d_out = nc.dram_tensor("ag_d_out", [64, 128, 2048], BF16)
    xres1 = nc.dram_tensor("xres1", [TC, D], F32)
    ln2ag_in = nc.dram_tensor("ln2ag_in", [2, D, TC], BF16)
    ln2ag_out = nc.dram_tensor("ln2ag_out", [NCORE, 2, D, TC], BF16)
    qT_d = nc.dram_tensor("qT_d", [NHC, 2, HD, S], BF16)
    kT_d = nc.dram_tensor("kT_d", [NHC, 2, HD, S], BF16)
    v_d = nc.dram_tensor("v_d", [NHC, 2, S // 128, 128, HD], BF16)
    oT_d = nc.dram_tensor("oT_d", [NHC, 2, HD, S], BF16)
    rs_in = nc.dram_tensor("rs_in", [T, D], F32)
    rs_out = nc.dram_tensor("rs_out", [TC, D], F32)

    rg_all = [list(range(8))]
    rg_grp = [[0, 1, 2, 3], [4, 5, 6, 7]]

    dbg = {}
    if stage == 11:
        dbg["qv"] = nc.dram_tensor("dbg_qv", [128, 16384], BF16, kind="ExternalOutput")
    if stage == 12:
        dbg["agg"] = nc.dram_tensor("dbg_agg", [16, 128, 1024], BF16, kind="ExternalOutput")
        dbg["agd"] = nc.dram_tensor("dbg_agd", [8, 128, 2048], BF16, kind="ExternalOutput")
    if stage == 1:
        dbg["qg"] = nc.dram_tensor("dbg_qg", [128, 128, 1024], BF16, kind="ExternalOutput")
        dbg["qu"] = nc.dram_tensor("dbg_qu", [128, 128, 1024], BF16, kind="ExternalOutput")
        dbg["qd"] = nc.dram_tensor("dbg_qd", [64, 128, 2048], BF16, kind="ExternalOutput")
    if stage == 2:
        dbg["x1"] = nc.dram_tensor("dbg_x1", [TC, D], F32, kind="ExternalOutput")
    if stage == 6:
        dbg["qlT"] = nc.dram_tensor("dbg_qlT", [128, 16, 512], BF16, kind="ExternalOutput")
        dbg["itf"] = nc.dram_tensor("dbg_itf", [128, 4, 512], F32, kind="ExternalOutput")
        dbg["qitT"] = nc.dram_tensor("dbg_qitT", [128, 4, 512], BF16, kind="ExternalOutput")
        dbg["acc0"] = nc.dram_tensor("dbg_acc0", [128, 2048], F32, kind="ExternalOutput")
        dbg["xin0"] = nc.dram_tensor("dbg_xin0", [128, 2048], F32, kind="ExternalOutput")
        dbg["ln2"] = nc.dram_tensor("dbg_ln2", [NCORE, 2, D, TC], BF16, kind="ExternalOutput")
    if stage == 3:
        dbg["rhs"] = nc.dram_tensor("dbg_rhs", [2, 128, 16, 512], BF16, kind="ExternalOutput")
        dbg["whi"] = nc.dram_tensor("dbg_whi", [128, 16, 768], BF16, kind="ExternalOutput")
        dbg["ln2o"] = nc.dram_tensor("dbg_ln2o", [NCORE, 2, D, TC], BF16, kind="ExternalOutput")
        dbg["kT"] = nc.dram_tensor("dbg_kT", [NHC, 2, HD, S], BF16, kind="ExternalOutput")
        dbg["qTt"] = nc.dram_tensor("dbg_qT", [NHC, 2, HD, S], BF16, kind="ExternalOutput")
        dbg["v"] = nc.dram_tensor("dbg_v", [NHC, 2, S // 128, 128, HD], BF16, kind="ExternalOutput")
    if stage == 4:
        dbg["x2"] = nc.dram_tensor("dbg_x2", [TC, D], F32, kind="ExternalOutput")

    s_ld = bd.sem("s_ld")
    s_sc = bd.sem("s_sc")
    s_st = bd.sem("s_st")
    s_dve = bd.sem("s_dve")
    s_act = bd.sem("s_act")
    s_pe = bd.sem("s_pe")
    s_cc = bd.sem("s_cc")
    s_ph = bd.sem("s_ph")

    def barrier():
        r.wait("sy", s_ld)
        r.wait("sy", s_sc)
        r.wait("g", s_st)
        for e in ENGS:
            r.op(e, "nop", (s_ph, 1))
        val = s_ph.v
        for e in ENGS:
            r.wait(e, s_ph, val)

    g_stk = ExitStack()
    identf = bd.sb(g_stk, "identf", [128, 128], F32)
    identb = bd.sb(g_stk, "identb", [128, 128], BF16)

    lnsc = {
        "sm": bd.sb(g_stk, "ln_sm", [128, 1], F32)[:],
        "ssq": bd.sb(g_stk, "ln_ssq", [128, 1], F32)[:],
        "var": bd.sb(g_stk, "ln_var", [128, 1], F32)[:],
        "r0": bd.sb(g_stk, "ln_r0", [128, 1], F32)[:],
        "t1": bd.sb(g_stk, "ln_t1", [128, 1], F32)[:],
        "scr": bd.sb(g_stk, "ln_scr", [128, 2048], F32)[:],
    }

    identb_f = bd.sb(g_stk, "identb_f", [128, 128], F32)
    r.dma("sy", s_ld, out=identf[:], in_=identf_in)
    r.dma("sy", s_ld, out=identb_f[:], in_=identb_in)

    r.wait("v", s_ld)
    r.cp(identb[:], identb_f[:], inc=[(s_dve, 1)])
    r.wait("t", s_dve)

    with ExitStack() as ph:
        wf32 = bd.sb(ph, "w_f32", [128, 2048], F32)
        qbuf = bd.sb(ph, "w_q", [128, 16, 1024], BF16)
        stg = bd.sb(ph, "w_stage", [128, 16, 128], BF16)
        pst = bd.psum(ph, "w_pst", [128, 8, 128], BF16)
        qs = bd.make_quant_scratch(ph, 2048, "wqs", weights=True)

        jobs = [
            (wg_in, 8, 2048, ag_g_in, 16),
            (wu_in, 8, 2048, ag_u_in, 16),
            (wd_in, 16, 1024, ag_d_in, 8),
        ]
        _nob = 99
        for (w_in, G, Fw, ag_in, n_ob) in jobs:
            r.wait("sy", s_pe)
            wr = w_in.rearrange("(g p) f -> p g f", p=128)
            qv = qbuf[:].rearrange("p a b -> p (a b)").rearrange("p (g f) -> p g f", g=G)
            for g in range(G):
                r.wait("sy", s_dve)
                r.dma("sy", s_ld, out=wf32[:, 0:Fw], in_=wr[:, g, :])
                r.wait("v", s_ld)
                bd.emit_quant(qs, wf32[:, 0:Fw], qv[:, g, :],
                              Fw, 1.0, 6.0, True, s_dve, s_act)
                r.op("v", "nop", (s_dve, 1))
            if stage == 11:
                r.wait("g", s_dve)
                r.dma("g", s_st, out=dbg["qv"].ap(),
                      in_=qbuf[:].rearrange("p a b -> p (a b)"))
                r.wait("g", s_st)
                return bd.finish(dbg)
            r.wait("t", s_ld)
            r.wait("t", s_dve)
            for ob in range(min(n_ob, _nob)):
                for gg0 in range(0, G, 8):
                    ng = min(8, G - gg0)
                    r.wait("t", s_act)
                    for j in range(ng):
                        r.tr(pst[:, j, :], qv[:, gg0 + j, ob * 128:(ob + 1) * 128],
                             identb[:], inc=[(s_pe, 1)] if j == ng - 1 else None)
                    r.wait("s", s_pe)
                    r.wait("s", s_st)
                    r.scp(stg[:, gg0:gg0 + ng, :], pst[:, 0:ng, :], inc=[(s_act, 1)])
                r.wait("g", s_act)
                r.dma("g", s_st, out=ag_in.ap()[ob],
                      in_=stg[:, 0:G, :].rearrange("p g o -> p (g o)"))
            r.wait("v", s_pe)

        if stage == 12:
            r.wait("g", s_st)
            r.dma("g", s_st, out=dbg["agg"].ap(), in_=ag_g_in.ap())
            r.dma("g", s_st, out=dbg["agd"].ap(), in_=ag_d_in.ap())
            r.wait("g", s_st)
            return bd.finish(dbg)
        r.wait("g", s_st)
        for (ain, aout) in [(ag_g_in, ag_g_out), (ag_u_in, ag_u_out), (ag_d_in, ag_d_out)]:
            r.op("g", "collective_compute", (s_cc, 1), kind="AllGather",
                 op=Alu.bypass, replica_groups=rg_all,
                 ins=[ain.ap().opt()], outs=[aout.ap().opt()])
        cc_w = s_cc.v

    barrier()
    if stage == 1:
        r.wait("g", s_cc)
        r.dma("g", s_st, out=dbg["qg"].ap(), in_=ag_g_out.ap())
        r.dma("g", s_st, out=dbg["qu"].ap(), in_=ag_u_out.ap())
        r.dma("g", s_st, out=dbg["qd"].ap(), in_=ag_d_out.ap())
        r.wait("g", s_st)
        return bd.finish(dbg)

    def emit_ffn(call):
        P = f"c{call}"
        with ExitStack() as ph:
            xt = [bd.sb(ph, f"{P}_x{t}", [128, 2048], F32) for t in range(4)]
            lnb = bd.sb(ph, P + "_ln", [128, 2048], F32)
            dlt = bd.sb(ph, P + "_dlt", [128, 2048], F32)
            qn = bd.sb(ph, P + "_qn", [128, 2048], BF16)
            qlT = bd.sb(ph, P + "_qlT", [128, 16, 512], BF16)
            gT = bd.sb(ph, P + "_gT", [128, 16, 512], BF16)
            uT = bd.sb(ph, P + "_uT", [128, 16, 512], BF16)
            dT = bd.sb(ph, P + "_dT", [128, 4, 2048], BF16)
            sg = bd.sb(ph, P + "_sg", [128, 512], F32)
            itf = bd.sb(ph, P + "_itf", [128, 4, 512], F32)
            it_tm = bd.sb(ph, P + "_ittm", [128, 4, 512], F32)
            q_it = bd.sb(ph, P + "_qit", [128, 4, 512], BF16)
            q_itT = bd.sb(ph, P + "_qitT", [128, 4, 512], BF16)
            lhi = bd.sb(ph, P + "_lhi", [128, 2048], BF16)
            llo = bd.sb(ph, P + "_llo", [128, 2048], BF16)
            stg_ln = bd.sb(ph, P + "_stgln", [128, 4, 128], BF16)
            qs = bd.make_quant_scratch(ph, 1024, P + "qs")
            pG = bd.psum(ph, P + "_pG", [128, 512])
            pU = bd.psum(ph, P + "_pU", [128, 512])
            pD = bd.psum(ph, P + "_pD", [128, 2048])
            pTF = bd.psum(ph, P + "_pTF", [128, 512])
            pTB = bd.psum(ph, P + "_pTB", [128, 8, 128], BF16)

            r.wait("sy", s_cc, cc_w)

            for qq in range(4):
                for t in range(4):
                    gt = qq * 4 + t
                    rows = slice(gt * 128, (gt + 1) * 128)
                    r.wait("sy", s_dve)
                    r.wait("sy", s_pe)
                    r.wait("sy", s_st)
                    if call == 0:
                        r.dma("sy", s_ld, out=xt[t][:], in_=x_in[rows, :])
                        r.wait("v", s_ld)
                    elif stage == 6:
                        r.dma("sy", s_ld, out=xt[t][:], in_=xres1.ap()[rows, :])
                        r.wait("v", s_ld)
                    else:
                        r.dma("sy", s_ld, out=xt[t][:], in_=xres1.ap()[rows, :])
                        r.dma("sy", s_ld, out=dlt[:], in_=rs_out.ap()[rows, :])
                        r.wait("v", s_ld)
                        r.tt(xt[t][:], xt[t][:], dlt[:], Alu.add)
                    bd.emit_ln(lnsc, xt[t][:], lnb[:], 2048, s_dve, s_act)
                    for half in range(2):
                        bd.emit_quant(qs, lnb[:, half * 1024:(half + 1) * 1024],
                                      qn[:, half * 1024:(half + 1) * 1024],
                                      1024, 3.0, 7.5, False, s_dve, s_act)
                    r.op("v", "nop", (s_dve, 1))
                    r.wait("t", s_dve)
                    for hc in range(16):
                        j = hc % 4
                        if j == 0:
                            r.wait("t", s_act)
                        r.tr(pTB[:, j, :], qn[:, hc * 128:(hc + 1) * 128],
                             identb[:], inc=[(s_pe, 1)] if j == 3 else None)
                        if j == 3:
                            r.wait("s", s_pe)
                            r.scp(qlT[:, hc - 3:hc + 1, t * 128:(t + 1) * 128],
                                  pTB[:, 0:4, :], inc=[(s_act, 1)])
                if call == 1 and qq == 0 and stage == 6:
                    r.wait("g", s_act)
                    r.dma("g", s_st, out=dbg["qlT"].ap(), in_=qlT[:])
                _nog = 16
                for og in range(_nog):
                    og2, half = og // 2, og % 2
                    r.wait("sy", s_pe)
                    r.wait("sy", s_dve)
                    r.dma("sy", s_ld, out=gT[:],
                          in_=ag_g_out.ap()[og2 * 16:(og2 + 1) * 16, :, half * 512:(half + 1) * 512].rearrange("c p f -> p c f"))
                    r.dma("sy", s_ld, out=uT[:],
                          in_=ag_u_out.ap()[og2 * 16:(og2 + 1) * 16, :, half * 512:(half + 1) * 512].rearrange("c p f -> p c f"))
                    r.dma("sy", s_ld, out=dT[:],
                          in_=ag_d_out.ap()[og * 4:(og + 1) * 4].rearrange("c p f -> p c f"))
                    r.wait("t", s_ld)
                    r.wait("t", s_act)
                    for oc in range(4):
                        for hc in range(16):
                            r.mm(pG[:], gT[:, hc, oc * 128:(oc + 1) * 128],
                                 qlT[:, hc, :], start=(hc == 0), stop=(hc == 15))
                        for hc in range(16):
                            r.mm(pU[:], uT[:, hc, oc * 128:(oc + 1) * 128],
                                 qlT[:, hc, :], start=(hc == 0),
                                 stop=(hc == 15), inc=[(s_pe, 1)] if hc == 15 else None)
                        r.wait("s", s_pe)
                        r.act(sg[:], pG[:], Act.Silu, inc=[(s_act, 1)])
                        r.wait("v", s_act)
                        r.tt(itf[:, oc, :], sg[:], pU[:], Alu.mult,
                             inc=[(s_dve, 1)])
                        r.wait("t", s_dve)
                    if call == 1 and qq == 0 and og == 0 and stage == 6:
                        r.wait("g", s_dve)
                        r.dma("g", s_st, out=dbg["itf"].ap(), in_=itf[:])
                    for t in range(4):
                        for j in range(4):
                            r.tr(pTF[:, j * 128:(j + 1) * 128],
                                 itf[:, j, t * 128:(t + 1) * 128], identf[:],
                                 inc=[(s_pe, 1)] if j == 3 else None)
                        r.wait("v", s_pe)
                        r.cp(it_tm[:, t, :], pTF[:, :], inc=[(s_dve, 1)])
                        r.wait("t", s_dve)
                    for t in range(4):
                        bd.emit_quant(qs, it_tm[:, t, :], q_it[:, t, :], 512,
                                      3.0, 7.5, False, s_dve, s_act)
                    r.op("v", "nop", (s_dve, 1))
                    r.wait("t", s_dve)
                    for t in range(4):
                        r.wait("t", s_act)
                        for j in range(4):
                            r.tr(pTB[:, j, :], q_it[:, t, j * 128:(j + 1) * 128],
                                 identb[:], inc=[(s_pe, 1)] if j == 3 else None)
                        r.wait("s", s_pe)
                        r.scp(q_itT[:, 0:4, t * 128:(t + 1) * 128],
                              pTB[:, 0:4, :], inc=[(s_act, 1)])
                    r.wait("t", s_act)
                    if call == 1 and qq == 0 and og == 0 and stage == 6:
                        r.wait("g", s_act)
                        r.dma("g", s_st, out=dbg["qitT"].ap(), in_=q_itT[:])
                    for t in range(4):
                        for oc in range(4):
                            for dc in range(4):
                                r.mm(pD[:, dc * 512:(dc + 1) * 512],
                                     q_itT[:, oc, t * 128:(t + 1) * 128],
                                     dT[:, oc, dc * 512:(dc + 1) * 512],
                                     start=(oc == 0), stop=(oc == 3),
                                     inc=[(s_pe, 1)] if (oc == 3 and dc == 3) else None)
                        r.wait("v", s_pe)
                        if call == 1 and qq == 0 and og == 0 and t == 0 and stage == 6:
                            r.wait("g", s_dve)
                            r.dma("g", s_st, out=dbg["xin0"].ap(), in_=xt[0][:])
                            r.wait("v", s_st)
                        r.stt(xt[t][:], pD[:, :], 0.5 if og > 0 else 0.5, xt[t][:],
                              Alu.mult, Alu.add, inc=[(s_dve, 1)])
                        if call == 1 and qq == 0 and og == 0 and t == 0 and stage == 6:
                            r.wait("g", s_dve)
                            r.dma("g", s_st, out=dbg["acc0"].ap(), in_=xt[0][:])
                            r.wait("v", s_st)
                        r.wait("t", s_dve)
                for t in range(4):
                    gt = qq * 4 + t
                    rows = slice(gt * 128, (gt + 1) * 128)
                    r.wait("g", s_dve)
                    if call == 0:
                        r.dma("g", s_st, out=xres1.ap()[rows, :], in_=xt[t][:])
                        bd.emit_ln(lnsc, xt[t][:], lnb[:], 2048, s_dve, s_act)
                        r.cp(lhi[:], lnb[:])
                        r.tt(llo[:], lnb[:], lhi[:], Alu.subtract, inc=[(s_dve, 1)])
                        r.wait("t", s_dve)
                        for s2, buf in ((0, lhi), (1, llo)):
                            for hc in range(16):
                                j = hc % 4
                                if j == 0:
                                    r.wait("t", s_act)
                                r.tr(pTB[:, j, :], buf[:, hc * 128:(hc + 1) * 128],
                                     identb[:], inc=[(s_pe, 1)] if j == 3 else None)
                                if j == 3:
                                    r.wait("s", s_pe)
                                    r.wait("s", s_st)
                                    r.scp(stg_ln[:, 0:4, :], pTB[:, 0:4, :],
                                          inc=[(s_act, 1)])
                                    r.wait("g", s_act)
                                    for jj in range(4):
                                        r.dma("g", s_st,
                                              out=ln2ag_in.ap()[s2, (hc - 3 + jj) * 128:(hc - 2 + jj) * 128, gt * 128:(gt + 1) * 128],
                                              in_=stg_ln[:, jj, :])
                    else:
                        r.dma("g", s_st, out=y_out[rows, :], in_=xt[t][:])

    with ExitStack() as qph:
        emit_ffn(0)
        barrier()

        r.wait("g", s_st)
        r.op("g", "collective_compute", (s_cc, 1), kind="AllGather",
             op=Alu.bypass, replica_groups=rg_all,
             ins=[ln2ag_in.ap().opt()], outs=[ln2ag_out.ap().opt()])
        cc_ln2 = s_cc.v

        if stage == 2:
            r.wait("g", s_cc)
            r.dma("g", s_st, out=dbg["x1"].ap(), in_=xres1.ap())
            r.dma("g", s_st, out=dbg["ln2"].ap(), in_=ln2ag_out.ap())
            r.wait("g", s_st)
            return bd.finish(dbg)

        if stage == 6:
            barrier()
            emit_ffn(1)
            r.wait("g", s_st)
            return bd.finish(dbg)
        barrier()
        _skip_qkv = False
        with ExitStack() as ph:
            whi = bd.sb(ph, "qkv_whi", [128, 16, 768], BF16)
            wlo = bd.sb(ph, "qkv_wlo", [128, 16, 768], BF16)
            wf = bd.sb(ph, "qkv_wf", [128, 4, 768], F32)
            rhs0 = bd.sb(ph, "qkv_rhs0", [128, 16, 512], BF16)
            rhs1 = bd.sb(ph, "qkv_rhs1", [128, 16, 512], BF16)
            rhs = [rhs0, rhs1]
            ehi = bd.sb(ph, "qkv_ehi", [128, 512], BF16)
            elo = bd.sb(ph, "qkv_elo", [128, 512], BF16)
            ef = bd.sb(ph, "qkv_ef", [128, 512], F32)
            pQ = bd.psum(ph, "qkv_pQ", [128, 512])
            pV = bd.psum(ph, "qkv_pV", [128, 8, 128], BF16)
            for piece in range(4):
                r.wait("sy", s_dve)
                r.dma("sy", s_ld, out=wf[:],
                      in_=wqkv_in.rearrange("(c p) n -> p c n", p=128)[:, piece * 4:(piece + 1) * 4, :])
                r.wait("v", s_ld)
                r.cp(whi[:, piece * 4:(piece + 1) * 4, :], wf[:])
                r.tt(wlo[:, piece * 4:(piece + 1) * 4, :], wf[:],
                     whi[:, piece * 4:(piece + 1) * 4, :], Alu.subtract, inc=[(s_dve, 1)])
            r.op("v", "nop", (s_dve, 1))
            r.wait("t", s_dve)
            r.wait("sy", s_cc, cc_ln2)
            for tc in ([] if _skip_qkv else range(32)):
                sl, off = tc // 4, tc % 4
                r.wait("sy", s_pe)
                for s2 in range(2):
                    for hc in range(16):
                        r.dma("sy", s_ld,
                              out=rhs[s2][:, hc, :],
                              in_=ln2ag_out.ap()[sl, s2, hc * 128:(hc + 1) * 128, off * 512:(off + 1) * 512])
                r.wait("t", s_ld)
                if stage == 3 and tc == 0:
                    r.wait("g", s_ld)
                    r.dma("g", s_st, out=dbg["rhs"].ap()[0], in_=rhs[0][:])
                    r.dma("g", s_st, out=dbg["rhs"].ap()[1], in_=rhs[1][:])
                    r.dma("g", s_st, out=dbg["whi"].ap(), in_=whi[:])
                for cc in range(6):
                    hl, kind = cc % 2, cc // 2
                    b = tc // 16
                    wb = tc % 16
                    u = hl * 2 + b
                    col0 = kind * 256 + hl * 128
                    r.wait("t", s_dve)
                    r.wait("t", s_act)
                    for hc in range(16):
                        r.mm(pQ[:], whi[:, hc, col0:col0 + 128], rhs[0][:, hc, :],
                             start=(hc == 0), stop=False)
                        r.mm(pQ[:], whi[:, hc, col0:col0 + 128], rhs[1][:, hc, :],
                             start=False, stop=False)
                        r.mm(pQ[:], wlo[:, hc, col0:col0 + 128], rhs[0][:, hc, :],
                             start=False, stop=(hc == 15),
                             inc=[(s_pe, 1)] if hc == 15 else None)
                    r.wait("s", s_pe)
                    r.wait("v", s_pe)
                    if kind < 2:
                        dst = qT_d if kind == 0 else kT_d
                        r.scp(ehi[:], pQ[:], inc=[(s_act, 1)])
                        r.wait("v", s_act)
                        r.tt(elo[:], pQ[:], ehi[:], Alu.subtract, inc=[(s_dve, 1)])
                        r.wait("g", s_dve)
                        r.dma("g", s_st, out=dst.ap()[u, 0, :, wb * 512:(wb + 1) * 512], in_=ehi[:])
                        r.dma("g", s_st, out=dst.ap()[u, 1, :, wb * 512:(wb + 1) * 512], in_=elo[:])
                        r.wait("s", s_st)
                        r.wait("v", s_st)
                    else:
                        r.scp(ef[:], pQ[:], inc=[(s_act, 1)])
                        r.wait("v", s_act)
                        r.cp(ehi[:], ef[:])
                        r.tt(elo[:], ef[:], ehi[:], Alu.subtract, inc=[(s_dve, 1)])
                        r.wait("t", s_dve)
                        for s2, buf in ((0, ehi), (1, elo)):
                            for j in range(4):
                                r.tr(pV[:, j, :], buf[:, j * 128:(j + 1) * 128],
                                     identb[:], inc=[(s_pe, 1)] if j == 3 else None)
                            r.wait("s", s_pe)
                            r.scp(ef[:].bitcast(BF16)[:, 0:512], pV[:, 0:4, :].rearrange("p a b -> p (a b)"), inc=[(s_act, 1)])
                            r.wait("g", s_act)
                            for j in range(4):
                                r.dma("g", s_st,
                                      out=v_d.ap()[u, s2, wb * 4 + j, :, :],
                                      in_=ef[:].bitcast(BF16)[:, j * 128:(j + 1) * 128])
                            r.wait("t", s_st)
                            r.wait("v", s_st)

        if stage == 3:
            r.wait("g", s_st)
            r.dma("g", s_st, out=dbg["ln2o"].ap(), in_=ln2ag_out.ap())
            r.dma("g", s_st, out=dbg["kT"].ap(), in_=kT_d.ap())
            r.dma("g", s_st, out=dbg["qTt"].ap(), in_=qT_d.ap())
            r.dma("g", s_st, out=dbg["v"].ap(), in_=v_d.ap())
            r.wait("g", s_st)
            return bd.finish(dbg)

        barrier()
        with ExitStack() as ph:
            maskt = bd.sb(ph, "masks", [128, 4, 512], F32)
            kTh = bd.sb(ph, "a_kT", [128, 2, S], BF16)
            vvb = bd.sb(ph, "a_v", [128, 2, 64, 128], BF16)
            qtl = bd.sb(ph, "a_q", [128, 2, 128], BF16)
            Sb = bd.sb(ph, "a_S", [128, S], F32)
            Agrp = bd.sb(ph, "a_A", [128, 2, 2, S], BF16)
            ATb = [bd.sb(ph, f"a_AT{i}", [128, 512], BF16) for i in range(2)]
            mxc = bd.sb(ph, "a_mxc", [128, 1], F32)
            mrun = bd.sb(ph, "a_mrun", [128, 1], F32)
            negm = bd.sb(ph, "a_negm", [128, 1], F32)
            sums = bd.sb(ph, "a_sums", [128, 1], F32)
            rcp = bd.sb(ph, "a_rcp", [128, 1], F32)
            of32 = bd.sb(ph, "a_of", [128, 256], F32)
            ohi = bd.sb(ph, "a_ohi", [128, 256], BF16)
            olo = bd.sb(ph, "a_olo", [128, 256], BF16)
            pS = bd.psum(ph, "a_pS", [128, 2, 512])
            pAT = [bd.psum(ph, f"a_pAT{i}", [128, 1024], BF16) for i in range(2)]
            pO = bd.psum(ph, "a_pO", [128, 512])

            r.dma("sy", s_ld, out=maskt[:], in_=masks_in)
            for h in range(NHC):
                r.wait("sy", s_pe)
                for s2 in range(2):
                    r.dma("sy", s_ld, out=kTh[:, s2, :], in_=kT_d.ap()[h, s2])
                    for kt in range(64):
                        r.dma("sy", s_ld, out=vvb[:, s2, kt, :], in_=v_d.ap()[h, s2, kt])
                r.wait("t", s_ld)
                r.wait("v", s_ld)
                o_evac_mark = None
                for qg in range(32):
                    nch = (2 * qg + 1) // 4 + 1
                    for qti in range(2):
                        qt = 2 * qg + qti
                        r.wait("sy", s_pe)
                        r.dma("sy", s_sc, out=qtl[:, 0, :], in_=qT_d.ap()[h, 0, :, qt * 128:(qt + 1) * 128])
                        r.dma("sy", s_sc, out=qtl[:, 1, :], in_=qT_d.ap()[h, 1, :, qt * 128:(qt + 1) * 128])
                        r.wait("t", s_sc)
                        r.wait("t", s_dve)
                        evac_m = {}
                        for ci in range(nch):
                            bank = ci % 2
                            if ci >= 2:
                                r.wait("t", s_dve, evac_m[ci - 2])
                            ks = slice(ci * 512, (ci + 1) * 512)
                            r.mm(pS[:, bank, :], qtl[:, 0, :], kTh[:, 0, ks], start=True, stop=False)
                            r.mm(pS[:, bank, :], qtl[:, 0, :], kTh[:, 1, ks], start=False, stop=False)
                            r.mm(pS[:, bank, :], qtl[:, 1, :], kTh[:, 0, ks], start=False, stop=True,
                                 inc=[(s_pe, 1)])
                            r.wait("v", s_pe)
                            if ci == nch - 1:
                                r.tt(pS[:, bank, :], pS[:, bank, :], maskt[:, qt % 4, :], Alu.add)
                            r.ts(Sb[:, ks], pS[:, bank, :], ATT_SCALE, -3.0e38,
                                 op0=Alu.mult, op1=Alu.max, accum_out=mxc[:],
                                 inc=[(s_dve, 1)])
                            evac_m[ci] = s_dve.v
                            if ci == 0:
                                r.cp(mrun[:], mxc[:])
                            else:
                                r.tt(mrun[:], mrun[:], mxc[:], Alu.max)
                        r.ts(negm[:], mrun[:], -1.0, op0=Alu.mult, inc=[(s_dve, 1)])
                        r.wait("s", s_dve)
                        r.act(Sb[:, 0:nch * 512], Sb[:, 0:nch * 512], Act.Exp,
                              bias=negm[:], accum_out=sums[:], inc=[(s_act, 1)])
                        r.wait("v", s_act)
                        r.op("v", "reciprocal", out=rcp[:], in_=sums[:])
                        r.wait("v", s_pe)
                        r.ts(Agrp[:, qti, 0, 0:nch * 512], Sb[:, 0:nch * 512], rcp[:], op0=Alu.mult)
                        r.stt(Agrp[:, qti, 1, 0:nch * 512], Sb[:, 0:nch * 512], rcp[:],
                              Agrp[:, qti, 0, 0:nch * 512], Alu.mult, Alu.subtract,
                              inc=[(s_dve, 1)])
                    a_ready = s_dve.v
                    at_mark = {}
                    for kt in range(nch * 4):
                        slot = kt % 2
                        r.wait("t", s_dve, a_ready)
                        if kt >= 2:
                            r.wait("t", s_act, at_mark[kt - 2][0])
                        if kt == 0 and o_evac_mark is not None:
                            r.wait("t", s_act, o_evac_mark)
                        for s2 in range(2):
                            for qti in range(2):
                                r.tr(pAT[slot][:, s2 * 256 + qti * 128: s2 * 256 + (qti + 1) * 128],
                                     Agrp[:, qti, s2, kt * 128:(kt + 1) * 128], identb[:],
                                     inc=[(s_pe, 1)] if (s2 == 1 and qti == 1) else None)
                        r.wait("s", s_pe)
                        r.scp(ATb[slot][:], pAT[slot][:, 0:512], inc=[(s_act, 1)])
                        at_mark[kt] = (s_act.v,)
                        r.wait("t", s_act)
                        last = (kt == nch * 4 - 1)
                        r.mm(pO[:, 0:256], vvb[:, 0, kt, :], ATb[slot][:, 0:256], start=(kt == 0), stop=False)
                        r.mm(pO[:, 0:256], vvb[:, 0, kt, :], ATb[slot][:, 256:512], start=False, stop=False)
                        r.mm(pO[:, 0:256], vvb[:, 1, kt, :], ATb[slot][:, 0:256], start=False, stop=last,
                             inc=[(s_pe, 1)] if last else None)
                    r.wait("s", s_pe)
                    r.scp(of32[:], pO[:, 0:256], inc=[(s_act, 1)])
                    o_evac_mark = s_act.v
                    r.wait("v", s_act)
                    r.cp(ohi[:], of32[:])
                    r.tt(olo[:], of32[:], ohi[:], Alu.subtract, inc=[(s_dve, 1)])
                    r.wait("g", s_dve)
                    r.dma("g", s_st, out=oT_d.ap()[h, 0, :, qg * 256:(qg + 1) * 256], in_=ohi[:])
                    r.dma("g", s_st, out=oT_d.ap()[h, 1, :, qg * 256:(qg + 1) * 256], in_=olo[:])
                    r.wait("v", s_st)

        barrier()
        with ExitStack() as ph:
            wof = bd.sb(ph, "wo_f", [128, 2, 2048], F32)
            wohi = bd.sb(ph, "wo_hi", [128, 2, 2048], BF16)
            wolo = bd.sb(ph, "wo_lo", [128, 2, 2048], BF16)
            otl = bd.sb(ph, "wo_ot", [128, 2, 2, 128], BF16)
            rsb = bd.sb(ph, "wo_rs", [128, 2048], F32)
            pWo = bd.psum(ph, "wo_p", [128, 2048])
            r.dma("sy", s_ld, out=wof[:], in_=wo_in.rearrange("(c p) n -> p c n", p=128))
            r.wait("v", s_ld)
            r.cp(wohi[:], wof[:])
            r.tt(wolo[:], wof[:], wohi[:], Alu.subtract, inc=[(s_dve, 1)])
            r.wait("t", s_dve)
            for qt in range(128):
                b, wq = qt // 64, qt % 64
                r.wait("sy", s_pe)
                for hl in range(2):
                    for s2 in range(2):
                        r.dma("sy", s_sc, out=otl[:, hl, s2, :],
                              in_=oT_d.ap()[hl * 2 + b, s2, :, wq * 128:(wq + 1) * 128])
                r.wait("t", s_sc)
                if qt > 0:
                    r.wait("t", s_act)
                for dc in range(4):
                    cs = slice(dc * 512, (dc + 1) * 512)
                    for hl in range(2):
                        r.mm(pWo[:, cs], otl[:, hl, 0, :], wohi[:, hl, cs], start=(hl == 0), stop=False)
                        r.mm(pWo[:, cs], otl[:, hl, 0, :], wolo[:, hl, cs], start=False, stop=False)
                        r.mm(pWo[:, cs], otl[:, hl, 1, :], wohi[:, hl, cs], start=False, stop=(hl == 1),
                             inc=[(s_pe, 1)] if (hl == 1 and dc == 3) else None)
                r.wait("s", s_pe)
                r.scp(rsb[:], pWo[:], inc=[(s_act, 1)])
                r.wait("g", s_act)
                r.dma("g", s_st, out=rs_in.ap()[qt * 128:(qt + 1) * 128, :], in_=rsb[:])
            r.wait("g", s_st)
            r.op("g", "collective_compute", (s_cc, 1), kind="ReduceScatter",
                 op=Alu.add, replica_groups=rg_all,
                 ins=[rs_in.ap().opt()], outs=[rs_out.ap().opt()])
            r.wait("sy", s_cc)

        if stage == 4:
            with ExitStack() as ph:
                xa = bd.sb(ph, "x2a", [128, 2048], F32)
                xb = bd.sb(ph, "x2b", [128, 2048], F32)
                for gt in range(16):
                    rows = slice(gt * 128, (gt + 1) * 128)
                    r.dma("sy", s_ld, out=xa[:], in_=xres1.ap()[rows, :])
                    r.dma("sy", s_ld, out=xb[:], in_=rs_out.ap()[rows, :])
                    r.wait("v", s_ld)
                    r.tt(xa[:], xa[:], xb[:], Alu.add, inc=[(s_dve, 1)])
                    r.wait("g", s_dve)
                    r.dma("g", s_st, out=dbg["x2"].ap()[rows, :], in_=xa[:])
                    r.wait("sy", s_st)
            r.wait("g", s_st)
            return bd.finish(dbg)

        barrier()
        emit_ffn(1)
        r.wait("g", s_st)
        return bd.finish(dbg)


from concourse.bass_utils import run_bass_kernel_spmd

_NC_CACHE = {}


def _make_in_maps(x, wqkv, wo, wg, wu, wd):
    x2 = np.ascontiguousarray(np.asarray(x).reshape(16384, 2048), dtype=np.float32)
    masks = np.zeros((128, 4, 512), dtype=np.float32)
    j = np.arange(512)[None, None, :]
    v = np.arange(4)[None, :, None]
    p = np.arange(128)[:, None, None]
    masks[:] = np.where(j <= 128 * v + p, 0.0, -3.0e38)
    identf = np.eye(128, dtype=np.float32)
    identb = np.eye(128, dtype=np.float32)
    wqkv = np.asarray(wqkv); wo = np.asarray(wo)
    wg = np.asarray(wg); wu = np.asarray(wu); wd = np.asarray(wd)
    in_maps = []
    for c in range(8):
        h0 = 2 * c * 128
        wqkv_sl = np.concatenate(
            [wqkv[:, h0:h0 + 256],
             wqkv[:, 2048 + h0:2048 + h0 + 256],
             wqkv[:, 4096 + h0:4096 + h0 + 256]], axis=1)
        in_maps.append({
            "x_sl": np.ascontiguousarray(x2[c * 2048:(c + 1) * 2048]),
            "wqkv_sl": np.ascontiguousarray(wqkv_sl, dtype=np.float32),
            "wo_sl": np.ascontiguousarray(wo[h0:h0 + 256, :], dtype=np.float32),
            "wg_sl": np.ascontiguousarray(wg[c * 1024:(c + 1) * 1024, :], dtype=np.float32),
            "wu_sl": np.ascontiguousarray(wu[c * 1024:(c + 1) * 1024, :], dtype=np.float32),
            "wd_sl": np.ascontiguousarray(wd[:, c * 1024:(c + 1) * 1024], dtype=np.float32),
            "masks_i": masks,
            "identf_i": identf,
            "identb_i": identb,
        })
    return in_maps


def kernel(L0_attn_x, L0_attn_W_qkv, L0_attn_W_o, L0_ffn_w_gate, L0_ffn_w_up,
           L0_ffn_w_down):
    in_maps = _make_in_maps(L0_attn_x, L0_attn_W_qkv, L0_attn_W_o,
                            L0_ffn_w_gate, L0_ffn_w_up, L0_ffn_w_down)
    nc, _ = build(5)
    res = run_bass_kernel_spmd(nc, in_maps, core_ids=list(range(8)))
    out = np.concatenate([np.asarray(res.results[c]["y"]) for c in range(8)],
                         axis=0)
    return np.ascontiguousarray(out.reshape(2, 8192, 2048), dtype=np.float32)
